# version 08
# baseline (speedup 1.0000x reference)
import math
from contextlib import ExitStack

import numpy as np

import concourse.bass as bass
import concourse.mybir as mybir
from concourse.bass_utils import run_bass_kernel_spmd

F32 = mybir.dt.float32
BF16 = mybir.dt.bfloat16
I32 = mybir.dt.int32
AF = mybir.ActivationFunctionType
ALU = mybir.AluOpType
AX = mybir.AxisListType

S = 4096
T = 2048
D = 1024
DC = 8
G = 4
F = 4096
FC = 32
NQ = 4
NTK = 32
FT = 256
TWO_PI = float(2.0 * np.pi)


_NPHASE = 4
_DBG_STASH = False
_P2STOP = None


class Ct:
    def __init__(self, nc):
        self.h = nc.alloc_semaphore()
        self.n = 0

    def inc(self, instr, k=1):
        instr.then_inc(self.h, k)
        self.n += k
        return self.n


def build_nc():
    nc = bass.Bass("TRN2", target_bir_lowering=False)

    x_seq = nc.dram_tensor("x_seq", [S, D], F32, kind="ExternalInput").ap()
    a1c = nc.dram_tensor("a1c", [128, DC, S], F32, kind="ExternalInput").ap()
    h2col_d = nc.dram_tensor("h2col", [128, 1], F32, kind="ExternalInput").ap()
    idf_d = nc.dram_tensor("idf", [128, 128], F32, kind="ExternalInput").ap()
    rby_d = nc.dram_tensor("rby", [1, 3], F32, kind="ExternalInput").ap()
    wq_d = nc.dram_tensor("wq", [D, D], F32, kind="ExternalInput").ap()
    wk_d = nc.dram_tensor("wk", [D, D], F32, kind="ExternalInput").ap()
    wv_d = nc.dram_tensor("wv", [D, D], F32, kind="ExternalInput").ap()
    wo_d = nc.dram_tensor("wo", [D, D], F32, kind="ExternalInput").ap()
    w1_d = nc.dram_tensor("w1", [D, F], F32, kind="ExternalInput").ap()
    w2_d = nc.dram_tensor("w2", [F, D], F32, kind="ExternalInput").ap()
    bq_d = nc.dram_tensor("bq", [1, D], F32, kind="ExternalInput").ap()
    bk_d = nc.dram_tensor("bk", [1, D], F32, kind="ExternalInput").ap()
    bv_d = nc.dram_tensor("bv", [1, D], F32, kind="ExternalInput").ap()
    bo_d = nc.dram_tensor("bo", [1, D], F32, kind="ExternalInput").ap()
    b1_d = nc.dram_tensor("b1", [1, F], F32, kind="ExternalInput").ap()
    b2_d = nc.dram_tensor("b2", [1, D], F32, kind="ExternalInput").ap()
    gam_d = nc.dram_tensor("gam", [1, D], F32, kind="ExternalInput").ap()
    bet_d = nc.dram_tensor("bet", [1, D], F32, kind="ExternalInput").ap()
    y_out = nc.dram_tensor("y", [T, D], F32, kind="ExternalOutput").ap()

    _k = "ExternalOutput" if _DBG_STASH else "Internal"
    dbg0 = (nc.dram_tensor("dbg0", [128, 96], F32, kind="ExternalOutput").ap()
            if _DBG_STASH else None)
    xp_st = nc.dram_tensor("xp_st", [128, DC, T], F32, kind=_k).ap()
    xn_st = nc.dram_tensor("xn_st", [128, DC, T], BF16, kind=_k).ap()
    o_st = nc.dram_tensor("o_st", [128, DC, T], BF16, kind=_k).ap()
    g_st = nc.dram_tensor("g_st", [128, DC, T], F32, kind=_k).ap()
    sums_st = nc.dram_tensor("sums_st", [G, NQ, 2048], F32, kind="Internal").ap()
    sel_d = nc.dram_tensor("sel", [16, 1024], F32, kind="ExternalInput").ap()

    with ExitStack() as stack:
        def mk_sb(stk):
            def sb(name, shape, dtype=F32):
                return stk.enter_context(nc.sbuf_tensor(name, shape, dtype))
            return sb

        sb = mk_sb(stack)

        def psum(name, shape, dtype=F32):
            return stack.enter_context(nc.psum_tensor(name, shape, dtype))

        wst = sb("wst", [128, 2 * 2048], F32)
        stats = sb("stats", [128, 16], F32)
        scal = sb("scal", [128, 8], F32)
        srow = sb("srow", [1, 8], F32)
        biasT = sb("biasT", [128, 64], F32)
        b1T = sb("b1T", [128, FC], F32)
        bv_rep = sb("bv_rep", [128, D], F32)
        idf = sb("idf_sb", [128, 128], F32)
        h2col = sb("h2col_sb", [128, 1], F32)
        rby = sb("rby_sb", [1, 3], F32)
        onescol_f = sb("onescol_f", [128, 1], F32)
        onesrow_f = sb("onesrow_f", [1, 128], F32)
        invom_col = sb("invom_col", [128, 1], BF16)
        cpool = sb("cpool", [128, 4], F32)
        p0s = ExitStack()
        brow = mk_sb(p0s)("brow", [1, F], F32)

        psP = psum("psP", [128, 2048], F32)
        psO0 = psum("psO0", [128, 512], F32)
        psO1 = psum("psO1", [128, 512], F32)
        psS = psum("psS", [128, 512], F32)
        psR = psum("psR", [128, 512], F32)

        pS = Ct(nc)
        vS = Ct(nc)
        aS = Ct(nc)
        dIn = Ct(nc)
        dSt = Ct(nc)
        dRe = Ct(nc)
        dOut = Ct(nc)

        V = nc.vector
        TE = nc.tensor
        SC = nc.scalar
        SY = nc.sync

        def dmain(dst, src):
            dIn.inc(SY.dma_start(out=dst, in_=src), 16)
            return dIn.n

        def VC(instr):
            n = vS.inc(instr)
            V.wait_ge(vS.h, n)
            return n

        def AC(instr):
            n = aS.inc(instr)
            SC.wait_ge(aS.h, n)
            return n

        BQ, BK, BO, B2, GA, BE = 0, 8, 16, 24, 32, 40

        dmain(idf[:], idf_d)
        dmain(h2col[:], h2col_d)
        dmain(rby[:], rby_d)
        dmain(bv_rep[0:1, :], bv_d)
        V.wait_ge(dIn.h, dIn.n)
        V.memset(onescol_f[:], 1.0)
        V.memset(onesrow_f[:], 1.0)
        for j, v in enumerate([TWO_PI, -0.5, 1e-5]):
            V.memset(cpool[:, j:j + 1], v)
            nc.const_aps.register(F32, v, cpool[:, j:j + 1])

        VC(V.memset(srow[:], 0.0))
        r_, b_, y_ = rby[0:1, 0:1], rby[0:1, 1:2], rby[0:1, 2:3]
        VC(V.tensor_scalar(srow[0:1, 0:1], r_, 0.0125, 0.125, op0=ALU.mult, op1=ALU.add))
        VC(V.scalar_tensor_tensor(srow[0:1, 0:1], b_, 0.00625, srow[0:1, 0:1],
                                  op0=ALU.mult, op1=ALU.add))
        VC(V.scalar_tensor_tensor(srow[0:1, 0:1], y_, 0.01875, srow[0:1, 0:1],
                                  op0=ALU.mult, op1=ALU.add))
        VC(V.tensor_scalar(srow[0:1, 1:2], y_, float(0.1 * 2 * np.pi), TWO_PI,
                           op0=ALU.mult, op1=ALU.add))
        VC(V.reciprocal(srow[0:1, 1:2], srow[0:1, 1:2]))
        VC(V.tensor_scalar(srow[0:1, 2:3], r_, 0.1, 1.0, op0=ALU.mult, op1=ALU.add))
        VC(V.scalar_tensor_tensor(srow[0:1, 2:3], y_, 0.05, srow[0:1, 2:3],
                                  op0=ALU.mult, op1=ALU.add))
        VC(V.tensor_scalar(srow[0:1, 3:4], b_, 1e-6, 1e-5, op0=ALU.mult, op1=ALU.add))
        VC(V.tensor_scalar(srow[0:1, 4:5], y_, float(0.1 * math.sin(2.0 * math.pi * 0.5)),
                           1.0, op0=ALU.mult, op1=ALU.add))
        VC(V.reciprocal(srow[0:1, 4:5], srow[0:1, 4:5]))
        TE.wait_ge(vS.h, vS.n)
        TE.wait_ge(dIn.h, dIn.n)
        pS.inc(TE.matmul(psR[:, 0:8], onesrow_f[0:1, 0:128], srow[0:1, 0:8],
                         start=True, stop=True))
        V.wait_ge(pS.h, pS.n)
        VC(V.tensor_copy(scal[:], psR[:, 0:8]))
        SMD, I2PCY, GF, EPS2, INVOM = (scal[:, j:j + 1] for j in range(5))
        VC(V.tensor_copy(invom_col[:], INVOM))
        for half in range(2):
            pS.inc(TE.matmul(psP[:, half * 512:(half + 1) * 512], onesrow_f[0:1, 0:128],
                             bv_rep[0:1, half * 512:(half + 1) * 512],
                             start=True, stop=True))
        V.wait_ge(pS.h, pS.n)
        vS.inc(V.tensor_copy(bv_rep[:], psP[:, 0:1024]))

        def load_bias_T(dram, n_chunks, dst, dst_col):
            SY.wait_ge(pS.h, pS.n)
            cnt = dmain(brow[0:1, 0:n_chunks * 128], dram)
            TE.wait_ge(dIn.h, cnt)
            TE.wait_ge(vS.h, vS.n)
            for c in range(n_chunks):
                pc = c % 8
                last = TE.matmul(psR[:, pc:pc + 1], brow[0:1, c * 128:(c + 1) * 128],
                                 onesrow_f[0:1, 0:1], start=True, stop=True)
                if pc == 7 or c == n_chunks - 1:
                    pS.inc(last)
                    V.wait_ge(pS.h, pS.n)
                    lo = c - pc
                    vS.inc(V.tensor_copy(dst[:, dst_col + lo:dst_col + c + 1],
                                         psR[:, 0:pc + 1]))
                    TE.wait_ge(vS.h, vS.n)
        load_bias_T(bq_d, DC, biasT, BQ)
        load_bias_T(bk_d, DC, biasT, BK)
        load_bias_T(bo_d, DC, biasT, BO)
        load_bias_T(b2_d, DC, biasT, B2)
        load_bias_T(gam_d, DC, biasT, GA)
        load_bias_T(bet_d, DC, biasT, BE)
        load_bias_T(b1_d, FC, b1T, 0)
        VC(V.tensor_scalar_mul(biasT[:, BQ:BQ + 8], biasT[:, BQ:BQ + 8], SMD))
        VC(V.tensor_scalar_mul(biasT[:, GA:GA + 8], biasT[:, GA:GA + 8], GF))
        if _DBG_STASH:
            nc.all_engine_barrier()
            SY.dma_start(out=dbg0[:, 0:8], in_=scal[:]).then_inc(dIn.h, 16)
            SY.dma_start(out=dbg0[:, 8:72], in_=biasT[:]).then_inc(dIn.h, 16)
            SY.dma_start(out=dbg0[0:1, 72:80], in_=srow[0:1, :]).then_inc(dIn.h, 16)
            SY.dma_start(out=dbg0[:, 80:84], in_=cpool[:]).then_inc(dIn.h, 16)
            dIn.n += 64
            SY.wait_ge(dIn.h, dIn.n)
        nc.all_engine_barrier()
        p0s.close()
        if _NPHASE < 1:
            return nc

        stage1 = ExitStack()
        sb1 = mk_sb(stage1)
        xpT = sb1("xpT", [128, DC * S], BF16)
        p1s = ExitStack()
        sbp1 = mk_sb(p1s)
        pebuf = sbp1("pebuf", [128, 16 * 1024], F32)
        xstage = sbp1("xstage", [128, 2 * 1024], F32)
        a1st = sbp1("a1st", [128, 2 * 1024], F32)
        ubuf = sbp1("ubuf", [128, 1024], F32)
        kibuf = sbp1("kibuf", [128, 1024], I32)
        kfbuf = sbp1("kfbuf", [128, 1024], F32)
        xnbuf = sbp1("xnbuf", [128, 2 * 1024], F32)
        xpsb = sbp1("xpsb", [128, 2 * 1024], F32)
        xnbf = sbp1("xnbf", [128, 2 * 1024], BF16)
        v_r_done = {}
        a_sin_done = {}
        v_tile_done = {}
        p_tp_done = {}
        junk = kfbuf

        for half in range(2):
            tiles = list(range(half * 16, half * 16 + 16))
            if half == 1:
                SC.wait_ge(vS.h, vS.n)
            for i in tiles:
                sl = i % 2
                j = i % 16
                if i - 2 >= 0:
                    SY.wait_ge(vS.h, v_r_done[i - 2])
                cnt = dmain(a1st[:, sl * 1024:(sl + 1) * 1024],
                            a1c[:, :, i * 128:(i + 1) * 128])
                V.wait_ge(dIn.h, cnt)
                a1sl = a1st[:, sl * 1024:(sl + 1) * 1024]
                VC(V.tensor_scalar(ubuf[:], a1sl, I2PCY, h2col[:, 0:1],
                                   op0=ALU.mult, op1=ALU.add))
                VC(V.tensor_copy(kibuf[:], ubuf[:]))
                VC(V.tensor_copy(kfbuf[:], kibuf[:]))
                VC(V.tensor_sub(ubuf[:], ubuf[:], kfbuf[:]))
                VC(V.scalar_tensor_tensor(kfbuf[:], ubuf[:], 0.5, ubuf[:],
                                          op0=ALU.is_gt, op1=ALU.subtract))
                v_r_done[i] = VC(V.scalar_tensor_tensor(
                    ubuf[:], ubuf[:], -0.5, kfbuf[:],
                    op0=ALU.is_lt, op1=ALU.subtract))
                SC.wait_ge(vS.h, v_r_done[i])
                a_sin_done[i] = aS.inc(SC.activation(
                    pebuf[:, j * 1024:(j + 1) * 1024], ubuf[:], AF.Sin, scale=TWO_PI))
                V.wait_ge(aS.h, a_sin_done[i])
            for i in tiles:
                sl = i % 2
                j = i % 16
                prev = i - 2
                if prev >= 0:
                    SY.wait_ge(vS.h, v_tile_done.get(prev, 0))
                cx = dmain(xstage[:, sl * 1024:(sl + 1) * 1024],
                           x_seq[i * 128:(i + 1) * 128, :])
                xs = xstage[:, sl * 1024:(sl + 1) * 1024]
                st = stats[:, sl * 8:sl * 8 + 8]
                V.wait_ge(dIn.h, cx)
                VC(V.reduce_sum(st[:, 0:1], xs, axis=AX.X))
                VC(V.scalar_tensor_tensor(junk[:], xs, 1.0, xs, op0=ALU.bypass,
                                          op1=ALU.mult, accum_out=st[:, 1:2]))
                VC(V.tensor_scalar_mul(st[:, 2:3], st[:, 0:1], 1.0 / 1024.0))
                VC(V.tensor_scalar_mul(st[:, 3:4], st[:, 1:2], 1.0 / 1024.0))
                VC(V.tensor_mul(st[:, 4:5], st[:, 2:3], st[:, 2:3]))
                vv = VC(V.tensor_sub(st[:, 5:6], st[:, 3:4], st[:, 4:5]))
                SC.wait_ge(vS.h, vv)
                AC(SC.activation(st[:, 6:7], st[:, 5:6], AF.Ln, bias=1e-5))
                ard = AC(SC.activation(st[:, 7:8], st[:, 6:7], AF.Exp, scale=-0.5))
                V.wait_ge(aS.h, ard)
                vx = VC(V.tensor_scalar(xnbuf[:, sl * 1024:(sl + 1) * 1024], xs,
                                        st[:, 2:3], st[:, 7:8],
                                        op0=ALU.subtract, op1=ALU.mult))
                TE.wait_ge(vS.h, vx)
                for c in range(DC):
                    last = TE.transpose(
                        psP[:, sl * 1024 + c * 128: sl * 1024 + (c + 1) * 128],
                        xnbuf[:, sl * 1024 + c * 128: sl * 1024 + (c + 1) * 128],
                        idf[:])
                p_tp_done[i] = pS.inc(last)
                V.wait_ge(pS.h, p_tp_done[i])
                if half == 0 and i >= 2:
                    V.wait_ge(dSt.h, dSt.n)
                psl = psP[:, sl * 1024:(sl + 1) * 1024]
                VC(V.tensor_add(xpsb[:, sl * 1024:(sl + 1) * 1024], psl,
                                pebuf[:, j * 1024:(j + 1) * 1024]))
                V.tensor_copy(
                    xpT[:].rearrange("p (c s) -> p c s", s=S)[:, :, i * 128:(i + 1) * 128],
                    xpsb[:, sl * 1024:(sl + 1) * 1024].rearrange("p (c s) -> p c s", s=128))
                if half == 0:
                    ve = vS.inc(V.tensor_copy(xnbf[:, sl * 1024:(sl + 1) * 1024], psl))
                    SY.wait_ge(vS.h, ve)
                    dSt.inc(SY.dma_start(
                        out=xp_st[:, :, i * 128:(i + 1) * 128],
                        in_=xpsb[:, sl * 1024:(sl + 1) * 1024].rearrange(
                            "p (c s) -> p c s", s=128)), 16)
                    dSt.inc(SY.dma_start(
                        out=xn_st[:, :, i * 128:(i + 1) * 128],
                        in_=xnbf[:, sl * 1024:(sl + 1) * 1024].rearrange(
                            "p (c s) -> p c s", s=128)), 16)
                    v_tile_done[i] = ve
                else:
                    v_tile_done[i] = vS.inc(V.tensor_copy(xnbf[:, 0:8], psl[:, 0:8]))
        SY.wait_ge(dSt.h, dSt.n)
        nc.all_engine_barrier()
        p1s.close()
        if _NPHASE < 2:
            stage1.close()
            return nc

        p2s = ExitStack()
        sbp2 = mk_sb(p2s)
        KTg2 = sbp2("KTg2", [128, 2 * 2 * S], BF16)
        Vg2 = sbp2("Vg2", [128, 2 * NTK * 256], BF16)
        QTg2 = sbp2("QTg2", [128, 2 * 2 * T], BF16)
        expP = sbp2("expP", [128, 6 * 1024], BF16)
        sums_stg = sbp2("sums_stg", [1, 2048], F32)
        otstg = sbp2("otstg", [128, 1024], BF16)
        wqT = sbp2("wqT", [128, 2048], BF16)
        wkT = sbp2("wkT", [128, 2048], BF16)
        wvT = sbp2("wvT", [128, 2048], BF16)
        if _P2STOP == "palloc":
            nc.all_engine_barrier()
            p2s.close(); stage1.close()
            return nc
        wprep_state = {"par": 0, "evac": {}}

        def gsl(buf, slot, width):
            return slot * width

        def wslice_prep(wdram, rows_lo, dst_fn):
            par = wprep_state["par"]
            wprep_state["par"] += 1
            sl = par % 2
            if par >= 2:
                SY.wait_ge(aS.h, wprep_state["evac"][par - 2])
            cnt = dmain(wst[:, sl * 2048:(sl + 1) * 2048].rearrange(
                            "p (c k) -> p c k", k=1024),
                        wdram[rows_lo:rows_lo + 256, :].rearrange(
                            "(c p) k -> p c k", p=128))
            TE.wait_ge(dIn.h, cnt)
            TE.wait_ge(vS.h, vS.n)
            for c in range(2):
                TE.wait_ge(aS.h, aS.n)
                for kb in range(DC):
                    last = TE.transpose(
                        psP[:, kb * 128:(kb + 1) * 128],
                        wst[:, sl * 2048 + c * 1024 + kb * 128:
                            sl * 2048 + c * 1024 + (kb + 1) * 128],
                        idf[:])
                pS.inc(last)
                SC.wait_ge(pS.h, pS.n)
                for kb in range(DC):
                    a_last = SC.copy(dst_fn(kb, c), psP[:, kb * 128:(kb + 1) * 128])
                aS.inc(a_last)
            wprep_state["evac"][par] = aS.n

        fq = {"v": 0, "a": 0}

        def feed_wslice(wdram, rows_lo, dst):
            par = wprep_state["par"]
            wprep_state["par"] += 1
            sl = par % 2
            if par >= 2:
                SY.wait_ge(aS.h, wprep_state["evac"][par - 2])
            cnt = dmain(wst[:, sl * 2048:(sl + 1) * 2048].rearrange(
                            "p (c k) -> p c k", k=1024),
                        wdram[rows_lo:rows_lo + 256, :].rearrange(
                            "(c p) k -> p c k", p=128))
            yield
            for c in range(2):
                for sub in range(2):
                    TE.wait_ge(dIn.h, cnt)
                    TE.wait_ge(vS.h, fq["v"])
                    TE.wait_ge(aS.h, fq["a"])
                    for b in range(4):
                        kb = sub * 4 + b
                        last = TE.transpose(
                            psR[:, b * 128:(b + 1) * 128],
                            wst[:, sl * 2048 + c * 1024 + kb * 128:
                                sl * 2048 + c * 1024 + (kb + 1) * 128],
                            idf[:])
                    pS.inc(last)
                    SC.wait_ge(pS.h, pS.n)
                    for b in range(4):
                        kb = sub * 4 + b
                        a_last = SC.copy(dst[:, kb * 256 + c * 128: kb * 256 + (c + 1) * 128],
                                         psR[:, b * 128:(b + 1) * 128])
                    fq["a"] = aS.inc(a_last)
                    yield
            wprep_state["evac"][par] = aS.n

        def feed_proj(lhsT_fn, rhs_fn, n_cols, evac_fn):
            TE.wait_ge(vS.h, fq["v"])
            TE.wait_ge(aS.h, fq["a"])
            for kb in range(4):
                TE.matmul(psR[:, 0:n_cols], lhsT_fn(kb), rhs_fn(kb),
                          start=(kb == 0), stop=False, skip_group_check=True)
            yield
            for kb in range(4, DC):
                last = TE.matmul(psR[:, 0:n_cols], lhsT_fn(kb), rhs_fn(kb),
                                 start=False, stop=(kb == DC - 1),
                                 skip_group_check=True)
            mmark = pS.inc(last)
            V.wait_ge(pS.h, mmark)
            fq["v"] = vS.inc(evac_fn(psR))
            yield

        def make_group_feeder(gn):
            slot = gn % 2
            qo = gsl(QTg2, slot, 2 * T)
            ko = gsl(KTg2, slot, 2 * S)
            vo = gsl(Vg2, slot, NTK * 256)
            yield from feed_wslice(wq_d, gn * 256, wqT)
            yield from feed_wslice(wk_d, gn * 256, wkT)
            yield from feed_wslice(wv_d, gn * 256, wvT)
            for c in range(2):
                for n in range(NQ):
                    def evq(ps_t, c=c, n=n):
                        return V.tensor_scalar(
                            QTg2[:, qo + c * T + n * 512: qo + c * T + (n + 1) * 512],
                            ps_t[:, 0:512], SMD,
                            biasT[:, BQ + 2 * gn + c:BQ + 2 * gn + c + 1],
                            op0=ALU.mult, op1=ALU.add)
                    yield from feed_proj(
                        lambda kb, c=c: wqT[:, kb * 256 + c * 128:kb * 256 + (c + 1) * 128],
                        lambda kb, n=n: xpT[:, kb * S + n * 512: kb * S + (n + 1) * 512],
                        512, evq)
            for c in range(2):
                for n in range(8):
                    def evk(ps_t, c=c, n=n):
                        return V.tensor_scalar(
                            KTg2[:, ko + c * S + n * 512: ko + c * S + (n + 1) * 512],
                            ps_t[:, 0:512], 1.0,
                            biasT[:, BK + 2 * gn + c:BK + 2 * gn + c + 1],
                            op0=ALU.mult, op1=ALU.add)
                    yield from feed_proj(
                        lambda kb, c=c: wkT[:, kb * 256 + c * 128:kb * 256 + (c + 1) * 128],
                        lambda kb, n=n: xpT[:, kb * S + n * 512: kb * S + (n + 1) * 512],
                        512, evk)
            for tkc in range(NTK):
                def evv(ps_t, tkc=tkc):
                    return V.scalar_tensor_tensor(
                        Vg2[:, vo + tkc * 256: vo + (tkc + 1) * 256], ps_t[:, 0:256], 0.0,
                        bv_rep[:, gn * 256:(gn + 1) * 256], op0=ALU.add, op1=ALU.add)
                yield from feed_proj(
                    lambda kb, tkc=tkc: xpT[:, kb * S + tkc * 128:kb * S + (tkc + 1) * 128],
                    lambda kb: wvT[:, kb * 256:(kb + 1) * 256],
                    256, evv)

        wslice_prep(wq_d, 0, lambda kb, c: wqT[:, kb * 256 + c * 128: kb * 256 + (c + 1) * 128])
        wslice_prep(wk_d, 0, lambda kb, c: wkT[:, kb * 256 + c * 128: kb * 256 + (c + 1) * 128])
        wslice_prep(wv_d, 0, lambda kb, c: wvT[:, kb * 256 + c * 128: kb * 256 + (c + 1) * 128])
        TE.wait_ge(aS.h, wprep_state["evac"][wprep_state["par"] - 1])
        v_pr_done = {}
        pr_idx = [0]

        def proj_mm(lhsT_fn, rhs_fn, n_cols, evac_fn):
            idx = pr_idx[0]
            pr_idx[0] += 1
            ps_t = psO0 if idx % 2 == 0 else psO1
            TE.wait_ge(vS.h, v_pr_done.get(idx - 2, 0))
            for kb in range(DC):
                last = TE.matmul(ps_t[:, 0:n_cols], lhsT_fn(kb), rhs_fn(kb),
                                 start=(kb == 0), stop=(kb == DC - 1),
                                 skip_group_check=True)
            pS.inc(last)
            V.wait_ge(pS.h, pS.n)
            v_pr_done[idx] = vS.inc(evac_fn(ps_t))

        for c in range(2):
            for n in range(NQ):
                def evq0(ps_t, c=c, n=n):
                    return V.tensor_scalar(
                        QTg2[:, c * T + n * 512: c * T + (n + 1) * 512],
                        ps_t[:, 0:512], SMD, biasT[:, BQ + c:BQ + c + 1],
                        op0=ALU.mult, op1=ALU.add)
                proj_mm(lambda kb, c=c: wqT[:, kb * 256 + c * 128:kb * 256 + (c + 1) * 128],
                        lambda kb, n=n: xpT[:, kb * S + n * 512: kb * S + (n + 1) * 512],
                        512, evq0)
        for c in range(2):
            for n in range(8):
                def evk0(ps_t, c=c, n=n):
                    return V.tensor_scalar(
                        KTg2[:, c * S + n * 512: c * S + (n + 1) * 512],
                        ps_t[:, 0:512], 1.0, biasT[:, BK + c:BK + c + 1],
                        op0=ALU.mult, op1=ALU.add)
                proj_mm(lambda kb, c=c: wkT[:, kb * 256 + c * 128:kb * 256 + (c + 1) * 128],
                        lambda kb, n=n: xpT[:, kb * S + n * 512: kb * S + (n + 1) * 512],
                        512, evk0)
        for tkc in range(NTK):
            def evv0(ps_t, tkc=tkc):
                return V.scalar_tensor_tensor(
                    Vg2[:, tkc * 256:(tkc + 1) * 256], ps_t[:, 0:256], 0.0,
                    bv_rep[:, 0:256], op0=ALU.add, op1=ALU.add)
            proj_mm(lambda kb, tkc=tkc: xpT[:, kb * S + tkc * 128:kb * S + (tkc + 1) * 128],
                    lambda kb: wvT[:, kb * 256:(kb + 1) * 256],
                    256, evv0)

        d_ost_mark = {}
        d_sums_mark = {}
        for g in range(G):
            slot = g % 2
            qo = gsl(QTg2, slot, 2 * T)
            ko = gsl(KTg2, slot, 2 * S)
            vo = gsl(Vg2, slot, NTK * 256)
            TE.wait_ge(vS.h, vS.n)
            feeder = make_group_feeder(g + 1) if g + 1 < G else iter(())

            a_exp = {}
            p_qk = {}
            p_av = {}
            v_oev = {-1: 0}
            NI = 2 * NTK
            LAG = 3

            def emit_qk(ii, qo=qo, ko=ko):
                q, i = divmod(ii, NI)
                tk, p = divmod(i, 2)
                sl2 = ii % 2
                if ii >= 2:
                    TE.wait_ge(aS.h, a_exp[ii - 2])
                for e in range(2):
                    last = TE.matmul(
                        psP[:, sl2 * 1024 + e * 512: sl2 * 1024 + (e + 1) * 512],
                        KTg2[64 * e:64 * e + 64,
                             ko + p * S + tk * 128: ko + p * S + (tk + 1) * 128],
                        QTg2[64 * e:64 * e + 64,
                             qo + p * T + q * 512: qo + p * T + (q + 1) * 512],
                        start=True, stop=True, tile_position=(64 * e, 0),
                        skip_group_check=True)
                p_qk[ii] = pS.inc(last)

            def emit_av(jj_g, vo=vo, g=g):
                q, j = divmod(jj_g, NI)
                tkj, pj = divmod(j, 2)
                eb = (jj_g % 6) * 1024
                TE.wait_ge(aS.h, a_exp[jj_g])
                if tkj == 0:
                    TE.wait_ge(vS.h, v_oev[q - 1])
                pso = psO0 if pj == 0 else psO1
                for e in range(2):
                    TE.matmul(
                        pso[64 * e:64 * e + 64, 0:512],
                        Vg2[:, vo + tkj * 256 + (2 * pj + e) * 64:
                            vo + tkj * 256 + (2 * pj + e + 1) * 64],
                        expP[:, eb + e * 512: eb + (e + 1) * 512],
                        start=(tkj == 0), stop=(tkj == NTK - 1),
                        tile_position=(0, 64 * e), skip_group_check=True)
                for e in range(2):
                    hh = 2 * pj + e
                    last = TE.matmul(
                        psS[32 * hh:32 * hh + 1, 0:512], invom_col[:, 0:1],
                        expP[:, eb + e * 512: eb + (e + 1) * 512],
                        start=(tkj == 0), stop=(tkj == NTK - 1),
                        tile_position=(0, 32 * hh), skip_group_check=True)
                p_av[jj_g] = pS.inc(last)
                if j == NI - 1:
                    emit_epilogue(q, g)

            def emit_epilogue(q, g):
                V.wait_ge(pS.h, p_av[q * NI + NI - 1])
                V.wait_ge(dSt.h, d_ost_mark.get((g, q - 1), d_ost_mark.get((g - 1, NQ - 1), 0)))
                V.tensor_copy(otstg[:, 0:512], psO0[:, 0:512])
                V.tensor_copy(otstg[:, 512:1024], psO1[:, 0:512])
                for hh in range(4):
                    last = V.tensor_copy(sums_stg[0:1, hh * 512:(hh + 1) * 512],
                                         psS[32 * hh:32 * hh + 1, 0:512])
                v_oev[q] = vS.inc(last)
                SY.wait_ge(vS.h, v_oev[q])
                dSt.inc(SY.dma_start(
                    out=o_st[:, 2 * g:2 * g + 2, q * 512:(q + 1) * 512],
                    in_=otstg[:, 0:1024].rearrange("p (c s) -> p c s", s=512)), 16)
                dSt.inc(SY.dma_start(out=sums_st[g:g + 1, q, :],
                                     in_=sums_stg[0:1, 0:2048]), 16)
                d_ost_mark[(g, q)] = dSt.n

            def emit_exp(ii):
                sl2 = ii % 2
                SC.wait_ge(pS.h, p_qk[ii])
                if ii >= 6:
                    SC.wait_ge(pS.h, p_av[ii - 6])
                a_exp[ii] = aS.inc(SC.activation(
                    expP[:, (ii % 6) * 1024:(ii % 6 + 1) * 1024],
                    psP[:, sl2 * 1024:(sl2 + 1) * 1024], AF.Exp))

            SC.wait_ge(pS.h, pS.n)
            NTOT = NQ * NI
            emit_qk(0)
            emit_qk(1)
            for ii in range(NTOT):
                if ii >= LAG:
                    emit_av(ii - LAG)
                if ii % 2 == 0:
                    next(feeder, None)
                emit_exp(ii)
                if ii + 2 < NTOT:
                    emit_qk(ii + 2)
            for jj_g in range(NTOT - LAG, NTOT):
                emit_av(jj_g)
            for _ in feeder:
                pass
            TE.wait_ge(vS.h, v_oev[NQ - 1])
        nc.all_engine_barrier()
        p2s.close()
        stage1.close()
        if _NPHASE < 3:
            return nc

        p3s = ExitStack()
        sbp3 = mk_sb(p3s)
        w1bf = sbp3("w1bf", [128, DC * F], BF16)
        w2bf = sbp3("w2bf", [128, FC * D], BF16)
        h1bf = sbp3("h1bf", [128, FC * FT], BF16)
        xnt = sbp3("xnt", [128, 2 * DC * FT], BF16)
        gt = sbp3("gt", [128, DC * FT], F32)
        wc = {}
        for idx in range(2 * DC):
            kb, hhalf = idx // 2, idx % 2
            sl = idx % 2
            if idx >= 2:
                SY.wait_ge(vS.h, wc[idx - 2])
            cnt = dmain(wst[:, sl * 2048:sl * 2048 + 2048],
                        w1_d[kb * 128:(kb + 1) * 128, hhalf * 2048:hhalf * 2048 + 2048])
            V.wait_ge(dIn.h, cnt)
            wc[idx] = vS.inc(V.tensor_copy(
                w1bf[:, kb * F + hhalf * 2048: kb * F + hhalf * 2048 + 2048],
                wst[:, sl * 2048:sl * 2048 + 2048]))
        wc2 = {}
        for kb in range(FC):
            sl = kb % 2
            SY.wait_ge(vS.h, wc2[kb - 2] if kb >= 2 else wc[2 * DC - 1])
            cnt = dmain(wst[:, sl * 2048:sl * 2048 + 1024],
                        w2_d[kb * 128:(kb + 1) * 128, :])
            V.wait_ge(dIn.h, cnt)
            wc2[kb] = vS.inc(V.tensor_copy(w2bf[:, kb * D:(kb + 1) * D],
                                           wst[:, sl * 2048:sl * 2048 + 1024]))
        SY.wait_ge(dSt.h, dSt.n)
        nc.all_engine_barrier()

        v_h1ev = {}
        a_gev = {}
        d_gst = {}
        d_xnt = {}
        p_h1last = {}
        for tc in range(T // FT):
            sl = tc % 2
            if tc >= 2:
                SY.wait_ge(pS.h, p_h1last[tc - 2])
            d_xnt[tc] = dmain(xnt[:, sl * DC * FT:(sl + 1) * DC * FT],
                              xn_st[:, :, tc * FT:(tc + 1) * FT])
            TE.wait_ge(dIn.h, d_xnt[tc])
            for ic in range(FC):
                ph = (psO0 if ic % 2 == 0 else psO1)[:, 0:256]
                TE.wait_ge(vS.h, v_h1ev.get((tc, ic - 2), 0))
                for kb in range(DC):
                    last = TE.matmul(
                        ph, w1bf[:, kb * F + ic * 128: kb * F + (ic + 1) * 128],
                        xnt[:, sl * DC * FT + kb * FT: sl * DC * FT + (kb + 1) * FT],
                        start=(kb == 0), stop=(kb == DC - 1), skip_group_check=True)
                pS.inc(last)
                V.wait_ge(pS.h, pS.n)
                v_h1ev[(tc, ic)] = vS.inc(V.tensor_scalar(
                    h1bf[:, ic * FT:(ic + 1) * FT], ph, b1T[:, ic:ic + 1], 0.0,
                    op0=ALU.add, op1=ALU.max))
                if ic == FC - 1:
                    p_h1last[tc] = pS.n
            for jc in range(DC):
                ph = (psS if jc % 2 == 0 else psR)[:, 0:256]
                TE.wait_ge(aS.h, a_gev.get((tc, jc - 2), 0))
                if jc == 0:
                    TE.wait_ge(vS.h, v_h1ev[(tc, FC - 1)])
                for kb in range(FC):
                    last = TE.matmul(
                        ph, w2bf[:, kb * D + jc * 128: kb * D + (jc + 1) * 128],
                        h1bf[:, kb * FT:(kb + 1) * FT],
                        start=(kb == 0), stop=(kb == FC - 1), skip_group_check=True)
                pS.inc(last)
                SC.wait_ge(pS.h, pS.n)
                if tc >= 1:
                    SC.wait_ge(dSt.h, d_gst[tc - 1])
                a_gev[(tc, jc)] = aS.inc(SC.activation(
                    gt[:, jc * FT:(jc + 1) * FT], ph, AF.Silu,
                    bias=biasT[:, B2 + jc:B2 + jc + 1]))
            SY.wait_ge(aS.h, a_gev[(tc, DC - 1)])
            dSt.inc(SY.dma_start(out=g_st[:, :, tc * FT:(tc + 1) * FT],
                                 in_=gt[:].rearrange("p (c s) -> p c s", s=FT)), 16)
            d_gst[tc] = dSt.n
        SY.wait_ge(dSt.h, dSt.n)
        nc.all_engine_barrier()
        p3s.close()
        if _NPHASE < 4:
            return nc

        p4s = ExitStack()
        sbp4 = mk_sb(p4s)
        woT = sbp4("woT", [128, DC * 1024], BF16)
        xpt4 = sbp4("xpt4", [128, DC * 512], F32)
        ot4 = sbp4("ot4", [128, DC * 512], BF16)
        otn = sbp4("otn", [128, DC * 512], BF16)
        sel_sb = sbp4("sel_sb", [16, 1024], F32)
        sumsb = sbp4("sumsb", [16, 512], F32)
        recipb = sbp4("recipb", [16, 512], F32)
        gt4 = sbp4("gt4", [128, DC * 512], F32)
        abuf = sbp4("abuf", [128, DC * 512], F32)
        asq = sbp4("asq", [128, DC * 512], F32)
        sbuf_s = sbp4("sbuf_s", [128, DC * 512], F32)
        st4 = sbp4("st4", [1, 6 * 512], F32)
        xt4 = sbp4("xt4", [128, 2 * 1024], F32)
        outb = sbp4("outb", [128, 2 * 1024], F32)
        for rg in range(4):
            if rg > 0:
                SC.wait_ge(vS.h, vS.n)
            wslice_prep(wo_d, rg * 256,
                        lambda kb, c, rg=rg: woT[:, kb * 1024 + rg * 256 + c * 128:
                                                 kb * 1024 + rg * 256 + (c + 1) * 128])
        SY.wait_ge(dSt.h, dSt.n)
        nc.all_engine_barrier()

        out_dma_mark = {}
        cnt_sel = dmain(sel_sb[:], sel_d)
        for q in range(NQ):
            if q > 0:
                SY.wait_ge(pS.h, pS.n)
                SY.wait_ge(vS.h, vS.n)
            c_o = dRe.n + 16
            dRe.inc(SY.dma_start(out=ot4[:].rearrange("p (c s) -> p c s", s=512),
                                 in_=o_st[:, :, q * 512:(q + 1) * 512]), 16)
            for g2 in range(G):
                dRe.inc(SY.dma_start(
                    out=sumsb[4 * g2:4 * g2 + 4, :],
                    in_=sums_st[g2, q, :].rearrange("(j t) -> j t", t=512)), 16)
            c_s = dRe.n
            dRe.inc(SY.dma_start(out=xpt4[:].rearrange("p (c s) -> p c s", s=512),
                                 in_=xp_st[:, :, q * 512:(q + 1) * 512]), 16)
            dRe.inc(SY.dma_start(out=gt4[:].rearrange("p (c s) -> p c s", s=512),
                                 in_=g_st[:, :, q * 512:(q + 1) * 512]), 16)
            d_re = dRe.n
            V.wait_ge(dRe.h, c_s)
            v_rec4 = VC(V.reciprocal(recipb[:], sumsb[:]))
            TE.wait_ge(vS.h, v_rec4)
            TE.wait_ge(dRe.h, c_o)
            TE.wait_ge(dIn.h, cnt_sel)
            v_nrm = {}
            for kb in range(DC):
                pn = psR if kb % 2 == 0 else psS
                TE.wait_ge(vS.h, v_nrm.get(kb - 2, 0))
                last = TE.matmul(pn[:, 0:512], sel_sb[0:16, kb * 128:(kb + 1) * 128],
                                 recipb[:], start=True, stop=True,
                                 skip_group_check=True)
                pS.inc(last)
                V.wait_ge(pS.h, pS.n)
                v_nrm[kb] = vS.inc(V.tensor_mul(
                    otn[:, kb * 512:(kb + 1) * 512],
                    ot4[:, kb * 512:(kb + 1) * 512], pn[:, 0:512]))
            v_aev = {}
            for jc in range(DC):
                ph = psO0 if jc % 2 == 0 else psO1
                TE.wait_ge(vS.h, v_aev.get(jc - 2, 0))
                if jc == 0:
                    TE.wait_ge(vS.h, v_nrm[DC - 1])
                for kb in range(DC):
                    last = TE.matmul(
                        ph[:, 0:512],
                        woT[:, kb * 1024 + jc * 128: kb * 1024 + (jc + 1) * 128],
                        otn[:, kb * 512:(kb + 1) * 512],
                        start=(kb == 0), stop=(kb == DC - 1), skip_group_check=True)
                pS.inc(last)
                V.wait_ge(pS.h, pS.n)
                if jc == 0:
                    V.wait_ge(dRe.h, d_re)
                v_aev[jc] = VC(V.scalar_tensor_tensor(
                    abuf[:, jc * 512:(jc + 1) * 512], ph[:, 0:512],
                    biasT[:, BO + jc:BO + jc + 1], xpt4[:, jc * 512:(jc + 1) * 512],
                    op0=ALU.add, op1=ALU.add))
            v_asq = VC(V.tensor_mul(asq[:], abuf[:], abuf[:]))
            TE.wait_ge(vS.h, v_asq)
            for kb in range(DC):
                TE.matmul(psS[0:1, 0:512], onescol_f[:, 0:1],
                          abuf[:, kb * 512:(kb + 1) * 512],
                          start=(kb == 0), stop=(kb == DC - 1),
                          tile_position=(0, 0), skip_group_check=True)
                last = TE.matmul(psS[32:33, 0:512], onescol_f[:, 0:1],
                                 asq[:, kb * 512:(kb + 1) * 512],
                                 start=(kb == 0), stop=(kb == DC - 1),
                                 tile_position=(0, 32), skip_group_check=True)
            p_stats = pS.inc(last)
            V.wait_ge(pS.h, p_stats)
            VC(V.tensor_scalar_mul(st4[0:1, 0:512], psS[0:1, 0:512], 1.0 / 1024.0))
            VC(V.tensor_scalar_mul(st4[0:1, 512:1024], psS[32:33, 0:512], 1.0 / 1024.0))
            VC(V.tensor_mul(st4[0:1, 1024:1536], st4[0:1, 0:512], st4[0:1, 0:512]))
            vv = VC(V.tensor_sub(st4[0:1, 1024:1536], st4[0:1, 512:1024],
                                 st4[0:1, 1024:1536]))
            SC.wait_ge(vS.h, vv)
            AC(SC.activation(st4[0:1, 1536:2048], st4[0:1, 1024:1536], AF.Ln,
                             bias=EPS2[0:1, 0:1]))
            a_r = AC(SC.activation(st4[0:1, 2048:2560], st4[0:1, 1536:2048],
                                   AF.Exp, scale=-0.5))
            V.wait_ge(aS.h, a_r)
            v_mr = VC(V.tensor_mul(st4[0:1, 2560:3072], st4[0:1, 0:512],
                                   st4[0:1, 2048:2560]))
            TE.wait_ge(vS.h, v_mr)
            for e in range(2):
                TE.matmul(psR[64 * e:64 * e + 64, 0:512], onesrow_f[0:1, 0:64],
                          st4[0:1, 2048:2560], start=True, stop=True,
                          tile_position=(0, 64 * e), skip_group_check=True)
                last = TE.matmul(psP[64 * e:64 * e + 64, 0:512], onesrow_f[0:1, 0:64],
                                 st4[0:1, 2560:3072], start=True, stop=True,
                                 tile_position=(0, 64 * e), skip_group_check=True)
            p_rep = pS.inc(last)
            V.wait_ge(pS.h, p_rep)
            for jc in range(DC):
                VC(V.tensor_mul(sbuf_s[:, jc * 512:(jc + 1) * 512],
                                abuf[:, jc * 512:(jc + 1) * 512], psR[:, 0:512]))
                VC(V.tensor_sub(sbuf_s[:, jc * 512:(jc + 1) * 512],
                                sbuf_s[:, jc * 512:(jc + 1) * 512], psP[:, 0:512]))
                VC(V.tensor_scalar(sbuf_s[:, jc * 512:(jc + 1) * 512],
                                   sbuf_s[:, jc * 512:(jc + 1) * 512],
                                   biasT[:, GA + jc:GA + jc + 1],
                                   biasT[:, BE + jc:BE + jc + 1],
                                   op0=ALU.mult, op1=ALU.add))
                VC(V.tensor_mul(sbuf_s[:, jc * 512:(jc + 1) * 512],
                                sbuf_s[:, jc * 512:(jc + 1) * 512],
                                gt4[:, jc * 512:(jc + 1) * 512]))
            v_sg = vS.n
            TE.wait_ge(vS.h, v_sg)
            for sub in range(4):
                i_tok = q * 512 + sub * 128
                sl = sub % 2
                if sub >= 2 or q > 0:
                    SY.wait_ge(vS.h, vS.n)
                cx = dmain(xt4[:, sl * 1024:(sl + 1) * 1024],
                           x_seq[i_tok:i_tok + 128, :])
                if sub >= 1 or q > 0:
                    TE.wait_ge(vS.h, vS.n)
                for c in range(DC):
                    last = TE.transpose(
                        psP[:, 1024 + c * 128: 1024 + (c + 1) * 128],
                        sbuf_s[:, c * 512 + sub * 128: c * 512 + sub * 128 + 128],
                        idf[:])
                p_tp = pS.inc(last)
                V.wait_ge(pS.h, p_tp)
                V.wait_ge(dIn.h, cx)
                sub_g = q * 4 + sub
                if sub_g >= 2:
                    V.wait_ge(dOut.h, out_dma_mark[sub_g - 2])
                v_out = vS.inc(V.tensor_add(outb[:, sl * 1024:(sl + 1) * 1024],
                                            psP[:, 1024:2048],
                                            xt4[:, sl * 1024:(sl + 1) * 1024]))
                SY.wait_ge(vS.h, v_out)
                dOut.inc(SY.dma_start(out=y_out[i_tok:i_tok + 128, :],
                                      in_=outb[:, sl * 1024:(sl + 1) * 1024]), 16)
                out_dma_mark[sub_g] = dOut.n
        SY.wait_ge(dOut.h, dOut.n)
        nc.all_engine_barrier()
        p4s.close()
    return nc


def _host_constants():
    i_even = np.arange(0, D, 2, dtype=np.float64)
    base = np.power(10000.0, 2.0 * i_even / D)
    ang0 = np.arange(S, dtype=np.float64)[:, None] / base[None, :]
    didx = np.arange(D) // 2
    full = ang0[:, didx]
    a1 = {}
    for halfswap in range(2):
        perm = (np.arange(S) if halfswap == 0
                else np.concatenate([np.arange(T, S), np.arange(0, T)]))
        arr = full[perm].T.reshape(DC, 128, S).transpose(1, 0, 2)
        a1[halfswap] = np.ascontiguousarray(arr.astype(np.float32))
    h2col = (0.25 * (np.arange(128) % 2)).astype(np.float32).reshape(128, 1)
    idf = np.eye(128, dtype=np.float32)
    sel = np.zeros((16, 1024), dtype=np.float32)
    for c in range(DC):
        for p in range(128):
            sel[2 * c + p // 64, c * 128 + p] = 1.0
    return a1, h2col, idf, sel


def kernel(L0_attn_x, L0_attn_rby, L0_attn_Wq, L0_attn_bq, L0_attn_Wk, L0_attn_bk,
           L0_attn_Wv, L0_attn_bv, L0_attn_Wo, L0_attn_bo, L0_attn_ln_gamma,
           L0_attn_ln_beta, L0_ffn_W1, L0_ffn_b1, L0_ffn_W2, L0_ffn_b2):
    x = np.ascontiguousarray(np.asarray(L0_attn_x, dtype=np.float32))
    a1, h2col, idf, sel = _host_constants()
    nc = build_nc()

    def row(v):
        return np.ascontiguousarray(np.asarray(v, np.float32).reshape(1, -1))

    shared = {
        "h2col": h2col, "idf": idf, "sel": sel,
        "rby": row(L0_attn_rby),
        "wq": np.ascontiguousarray(L0_attn_Wq, np.float32),
        "wk": np.ascontiguousarray(L0_attn_Wk, np.float32),
        "wv": np.ascontiguousarray(L0_attn_Wv, np.float32),
        "wo": np.ascontiguousarray(L0_attn_Wo, np.float32),
        "w1": np.ascontiguousarray(L0_ffn_W1, np.float32),
        "w2": np.ascontiguousarray(L0_ffn_W2, np.float32),
        "bq": row(L0_attn_bq), "bk": row(L0_attn_bk), "bv": row(L0_attn_bv),
        "bo": row(L0_attn_bo), "b1": row(L0_ffn_b1), "b2": row(L0_ffn_b2),
        "gam": row(L0_attn_ln_gamma), "bet": row(L0_attn_ln_beta),
    }
    in_maps = []
    for c in range(8):
        b, half = c // 2, c % 2
        own = x[b, half * T:(half + 1) * T]
        oth = x[b, (1 - half) * T:(2 - half) * T]
        m = dict(shared)
        m["x_seq"] = np.ascontiguousarray(np.concatenate([own, oth], axis=0))
        m["a1c"] = a1[half]
        in_maps.append(m)

    res = run_bass_kernel_spmd(nc, in_maps, core_ids=list(range(8)))
    out = np.empty((4, S, D), dtype=np.float32)
    for c in range(8):
        b, half = c // 2, c % 2
        out[b, half * T:(half + 1) * T] = res.results[c]["y"]
    return out


# version 10
# speedup vs baseline: 1.2086x; 1.2086x over previous
import math
from contextlib import ExitStack

import numpy as np

import concourse.bass as bass
import concourse.mybir as mybir
from concourse.bass_utils import run_bass_kernel_spmd

F32 = mybir.dt.float32
BF16 = mybir.dt.bfloat16
I32 = mybir.dt.int32
AF = mybir.ActivationFunctionType
ALU = mybir.AluOpType
AX = mybir.AxisListType

S = 4096
T = 2048
D = 1024
DC = 8
G = 4
F = 4096
FC = 32
NQ = 4
NTK = 32
FT = 256
TWO_PI = float(2.0 * np.pi)


_NPHASE = 4
_DBG_STASH = False
_P2STOP = None


class Ct:
    def __init__(self, nc):
        self.h = nc.alloc_semaphore()
        self.n = 0

    def inc(self, instr, k=1):
        instr.then_inc(self.h, k)
        self.n += k
        return self.n


def build_nc():
    nc = bass.Bass("TRN2", target_bir_lowering=False)

    x_seq = nc.dram_tensor("x_seq", [S, D], F32, kind="ExternalInput").ap()
    a1c = nc.dram_tensor("a1c", [128, DC, S], F32, kind="ExternalInput").ap()
    h2col_d = nc.dram_tensor("h2col", [128, 1], F32, kind="ExternalInput").ap()
    idf_d = nc.dram_tensor("idf", [128, 128], F32, kind="ExternalInput").ap()
    rby_d = nc.dram_tensor("rby", [1, 3], F32, kind="ExternalInput").ap()
    wq_d = nc.dram_tensor("wq", [D, D], F32, kind="ExternalInput").ap()
    wk_d = nc.dram_tensor("wk", [D, D], F32, kind="ExternalInput").ap()
    wv_d = nc.dram_tensor("wv", [D, D], F32, kind="ExternalInput").ap()
    wo_d = nc.dram_tensor("wo", [D, D], F32, kind="ExternalInput").ap()
    w1_d = nc.dram_tensor("w1", [D, F], F32, kind="ExternalInput").ap()
    w2_d = nc.dram_tensor("w2", [F, D], F32, kind="ExternalInput").ap()
    bq_d = nc.dram_tensor("bq", [1, D], F32, kind="ExternalInput").ap()
    bk_d = nc.dram_tensor("bk", [1, D], F32, kind="ExternalInput").ap()
    bv_d = nc.dram_tensor("bv", [1, D], F32, kind="ExternalInput").ap()
    bo_d = nc.dram_tensor("bo", [1, D], F32, kind="ExternalInput").ap()
    b1_d = nc.dram_tensor("b1", [1, F], F32, kind="ExternalInput").ap()
    b2_d = nc.dram_tensor("b2", [1, D], F32, kind="ExternalInput").ap()
    gam_d = nc.dram_tensor("gam", [1, D], F32, kind="ExternalInput").ap()
    bet_d = nc.dram_tensor("bet", [1, D], F32, kind="ExternalInput").ap()
    y_out = nc.dram_tensor("y", [T, D], F32, kind="ExternalOutput").ap()

    _k = "ExternalOutput" if _DBG_STASH else "Internal"
    dbg0 = (nc.dram_tensor("dbg0", [128, 96], F32, kind="ExternalOutput").ap()
            if _DBG_STASH else None)
    xp_st = nc.dram_tensor("xp_st", [128, DC, T], F32, kind=_k).ap()
    xn_st = nc.dram_tensor("xn_st", [128, DC, T], BF16, kind=_k).ap()
    o_st = nc.dram_tensor("o_st", [128, DC, T], BF16, kind=_k).ap()
    g_st = nc.dram_tensor("g_st", [128, DC, T], F32, kind=_k).ap()
    sums_st = nc.dram_tensor("sums_st", [G, NQ, 2048], F32, kind="Internal").ap()
    sel_d = nc.dram_tensor("sel", [16, 1024], F32, kind="ExternalInput").ap()

    with ExitStack() as stack:
        def mk_sb(stk):
            def sb(name, shape, dtype=F32):
                return stk.enter_context(nc.sbuf_tensor(name, shape, dtype))
            return sb

        sb = mk_sb(stack)

        def psum(name, shape, dtype=F32):
            return stack.enter_context(nc.psum_tensor(name, shape, dtype))

        wst = sb("wst", [128, 2 * 2048], F32)
        stats = sb("stats", [128, 16], F32)
        scal = sb("scal", [128, 8], F32)
        srow = sb("srow", [1, 8], F32)
        biasT = sb("biasT", [128, 64], F32)
        b1T = sb("b1T", [128, FC], F32)
        bv_rep = sb("bv_rep", [128, D], F32)
        idf = sb("idf_sb", [128, 128], F32)
        h2col = sb("h2col_sb", [128, 1], F32)
        rby = sb("rby_sb", [1, 3], F32)
        onescol_f = sb("onescol_f", [128, 1], F32)
        onesrow_f = sb("onesrow_f", [1, 128], F32)
        invom_col = sb("invom_col", [128, 1], BF16)
        cpool = sb("cpool", [128, 4], F32)
        p0s = ExitStack()
        brow = mk_sb(p0s)("brow", [1, F], F32)

        psP = psum("psP", [128, 2048], F32)
        psO0 = psum("psO0", [128, 512], F32)
        psO1 = psum("psO1", [128, 512], F32)
        psS = psum("psS", [128, 512], F32)
        psR = psum("psR", [128, 512], F32)

        pS = Ct(nc)
        vS = Ct(nc)
        aS = Ct(nc)
        dIn = Ct(nc)
        dSt = Ct(nc)
        dRe = Ct(nc)
        dOut = Ct(nc)

        V = nc.vector
        TE = nc.tensor
        SC = nc.scalar
        SY = nc.sync

        def dmain(dst, src):
            dIn.inc(SY.dma_start(out=dst, in_=src), 16)
            return dIn.n

        def VC(instr):
            n = vS.inc(instr)
            V.wait_ge(vS.h, n)
            return n

        def AC(instr):
            n = aS.inc(instr)
            SC.wait_ge(aS.h, n)
            return n

        BQ, BK, BO, B2, GA, BE = 0, 8, 16, 24, 32, 40

        dmain(idf[:], idf_d)
        dmain(h2col[:], h2col_d)
        dmain(rby[:], rby_d)
        dmain(bv_rep[0:1, :], bv_d)
        V.wait_ge(dIn.h, dIn.n)
        V.memset(onescol_f[:], 1.0)
        V.memset(onesrow_f[:], 1.0)
        for j, v in enumerate([TWO_PI, -0.5, 1e-5]):
            V.memset(cpool[:, j:j + 1], v)
            nc.const_aps.register(F32, v, cpool[:, j:j + 1])

        VC(V.memset(srow[:], 0.0))
        r_, b_, y_ = rby[0:1, 0:1], rby[0:1, 1:2], rby[0:1, 2:3]
        VC(V.tensor_scalar(srow[0:1, 0:1], r_, 0.0125, 0.125, op0=ALU.mult, op1=ALU.add))
        VC(V.scalar_tensor_tensor(srow[0:1, 0:1], b_, 0.00625, srow[0:1, 0:1],
                                  op0=ALU.mult, op1=ALU.add))
        VC(V.scalar_tensor_tensor(srow[0:1, 0:1], y_, 0.01875, srow[0:1, 0:1],
                                  op0=ALU.mult, op1=ALU.add))
        VC(V.tensor_scalar(srow[0:1, 1:2], y_, float(0.1 * 2 * np.pi), TWO_PI,
                           op0=ALU.mult, op1=ALU.add))
        VC(V.reciprocal(srow[0:1, 1:2], srow[0:1, 1:2]))
        VC(V.tensor_scalar(srow[0:1, 2:3], r_, 0.1, 1.0, op0=ALU.mult, op1=ALU.add))
        VC(V.scalar_tensor_tensor(srow[0:1, 2:3], y_, 0.05, srow[0:1, 2:3],
                                  op0=ALU.mult, op1=ALU.add))
        VC(V.tensor_scalar(srow[0:1, 3:4], b_, 1e-6, 1e-5, op0=ALU.mult, op1=ALU.add))
        VC(V.tensor_scalar(srow[0:1, 4:5], y_, float(0.1 * math.sin(2.0 * math.pi * 0.5)),
                           1.0, op0=ALU.mult, op1=ALU.add))
        VC(V.reciprocal(srow[0:1, 4:5], srow[0:1, 4:5]))
        TE.wait_ge(vS.h, vS.n)
        TE.wait_ge(dIn.h, dIn.n)
        pS.inc(TE.matmul(psR[:, 0:8], onesrow_f[0:1, 0:128], srow[0:1, 0:8],
                         start=True, stop=True))
        V.wait_ge(pS.h, pS.n)
        VC(V.tensor_copy(scal[:], psR[:, 0:8]))
        SMD, I2PCY, GF, EPS2, INVOM = (scal[:, j:j + 1] for j in range(5))
        VC(V.tensor_copy(invom_col[:], INVOM))
        for half in range(2):
            pS.inc(TE.matmul(psP[:, half * 512:(half + 1) * 512], onesrow_f[0:1, 0:128],
                             bv_rep[0:1, half * 512:(half + 1) * 512],
                             start=True, stop=True))
        V.wait_ge(pS.h, pS.n)
        vS.inc(V.tensor_copy(bv_rep[:], psP[:, 0:1024]))

        def load_bias_T(dram, n_chunks, dst, dst_col):
            SY.wait_ge(pS.h, pS.n)
            cnt = dmain(brow[0:1, 0:n_chunks * 128], dram)
            TE.wait_ge(dIn.h, cnt)
            TE.wait_ge(vS.h, vS.n)
            for c in range(n_chunks):
                pc = c % 8
                last = TE.matmul(psR[:, pc:pc + 1], brow[0:1, c * 128:(c + 1) * 128],
                                 onesrow_f[0:1, 0:1], start=True, stop=True)
                if pc == 7 or c == n_chunks - 1:
                    pS.inc(last)
                    V.wait_ge(pS.h, pS.n)
                    lo = c - pc
                    vS.inc(V.tensor_copy(dst[:, dst_col + lo:dst_col + c + 1],
                                         psR[:, 0:pc + 1]))
                    TE.wait_ge(vS.h, vS.n)
        load_bias_T(bq_d, DC, biasT, BQ)
        load_bias_T(bk_d, DC, biasT, BK)
        load_bias_T(bo_d, DC, biasT, BO)
        load_bias_T(b2_d, DC, biasT, B2)
        load_bias_T(gam_d, DC, biasT, GA)
        load_bias_T(bet_d, DC, biasT, BE)
        load_bias_T(b1_d, FC, b1T, 0)
        VC(V.tensor_scalar_mul(biasT[:, BQ:BQ + 8], biasT[:, BQ:BQ + 8], SMD))
        VC(V.tensor_scalar_mul(biasT[:, GA:GA + 8], biasT[:, GA:GA + 8], GF))
        if _DBG_STASH:
            nc.all_engine_barrier()
            SY.dma_start(out=dbg0[:, 0:8], in_=scal[:]).then_inc(dIn.h, 16)
            SY.dma_start(out=dbg0[:, 8:72], in_=biasT[:]).then_inc(dIn.h, 16)
            SY.dma_start(out=dbg0[0:1, 72:80], in_=srow[0:1, :]).then_inc(dIn.h, 16)
            SY.dma_start(out=dbg0[:, 80:84], in_=cpool[:]).then_inc(dIn.h, 16)
            dIn.n += 64
            SY.wait_ge(dIn.h, dIn.n)
        nc.all_engine_barrier()
        p0s.close()
        if _NPHASE < 1:
            return nc

        stage1 = ExitStack()
        sb1 = mk_sb(stage1)
        xpT = sb1("xpT", [128, DC * S], BF16)
        p1s = ExitStack()
        sbp1 = mk_sb(p1s)
        pebuf = sbp1("pebuf", [128, 16 * 1024], F32)
        xstage = sbp1("xstage", [128, 2 * 1024], F32)
        a1st = sbp1("a1st", [128, 2 * 1024], F32)
        ubuf = sbp1("ubuf", [128, 1024], F32)
        kibuf = sbp1("kibuf", [128, 1024], I32)
        kfbuf = sbp1("kfbuf", [128, 1024], F32)
        xnbuf = sbp1("xnbuf", [128, 2 * 1024], F32)
        xpsb = sbp1("xpsb", [128, 2 * 1024], F32)
        xnbf = sbp1("xnbf", [128, 2 * 1024], BF16)
        v_r_done = {}
        a_sin_done = {}
        v_tile_done = {}
        p_tp_done = {}
        junk = kfbuf

        for half in range(2):
            tiles = list(range(half * 16, half * 16 + 16))
            if half == 1:
                SC.wait_ge(vS.h, vS.n)
            for i in tiles:
                sl = i % 2
                j = i % 16
                if i - 2 >= 0:
                    SY.wait_ge(vS.h, v_r_done[i - 2])
                cnt = dmain(a1st[:, sl * 1024:(sl + 1) * 1024],
                            a1c[:, :, i * 128:(i + 1) * 128])
                V.wait_ge(dIn.h, cnt)
                a1sl = a1st[:, sl * 1024:(sl + 1) * 1024]
                VC(V.tensor_scalar(ubuf[:], a1sl, I2PCY, h2col[:, 0:1],
                                   op0=ALU.mult, op1=ALU.add))
                VC(V.tensor_copy(kibuf[:], ubuf[:]))
                VC(V.tensor_copy(kfbuf[:], kibuf[:]))
                VC(V.tensor_sub(ubuf[:], ubuf[:], kfbuf[:]))
                VC(V.scalar_tensor_tensor(kfbuf[:], ubuf[:], 0.5, ubuf[:],
                                          op0=ALU.is_gt, op1=ALU.subtract))
                v_r_done[i] = VC(V.scalar_tensor_tensor(
                    ubuf[:], ubuf[:], -0.5, kfbuf[:],
                    op0=ALU.is_lt, op1=ALU.subtract))
                SC.wait_ge(vS.h, v_r_done[i])
                a_sin_done[i] = aS.inc(SC.activation(
                    pebuf[:, j * 1024:(j + 1) * 1024], ubuf[:], AF.Sin, scale=TWO_PI))
                V.wait_ge(aS.h, a_sin_done[i])
            for i in tiles:
                sl = i % 2
                j = i % 16
                prev = i - 2
                if prev >= 0:
                    SY.wait_ge(vS.h, v_tile_done.get(prev, 0))
                cx = dmain(xstage[:, sl * 1024:(sl + 1) * 1024],
                           x_seq[i * 128:(i + 1) * 128, :])
                xs = xstage[:, sl * 1024:(sl + 1) * 1024]
                st = stats[:, sl * 8:sl * 8 + 8]
                V.wait_ge(dIn.h, cx)
                VC(V.reduce_sum(st[:, 0:1], xs, axis=AX.X))
                VC(V.scalar_tensor_tensor(junk[:], xs, 1.0, xs, op0=ALU.bypass,
                                          op1=ALU.mult, accum_out=st[:, 1:2]))
                VC(V.tensor_scalar_mul(st[:, 2:3], st[:, 0:1], 1.0 / 1024.0))
                VC(V.tensor_scalar_mul(st[:, 3:4], st[:, 1:2], 1.0 / 1024.0))
                VC(V.tensor_mul(st[:, 4:5], st[:, 2:3], st[:, 2:3]))
                vv = VC(V.tensor_sub(st[:, 5:6], st[:, 3:4], st[:, 4:5]))
                SC.wait_ge(vS.h, vv)
                AC(SC.activation(st[:, 6:7], st[:, 5:6], AF.Ln, bias=1e-5))
                ard = AC(SC.activation(st[:, 7:8], st[:, 6:7], AF.Exp, scale=-0.5))
                V.wait_ge(aS.h, ard)
                vx = VC(V.tensor_scalar(xnbuf[:, sl * 1024:(sl + 1) * 1024], xs,
                                        st[:, 2:3], st[:, 7:8],
                                        op0=ALU.subtract, op1=ALU.mult))
                TE.wait_ge(vS.h, vx)
                for c in range(DC):
                    last = TE.transpose(
                        psP[:, sl * 1024 + c * 128: sl * 1024 + (c + 1) * 128],
                        xnbuf[:, sl * 1024 + c * 128: sl * 1024 + (c + 1) * 128],
                        idf[:])
                p_tp_done[i] = pS.inc(last)
                V.wait_ge(pS.h, p_tp_done[i])
                if half == 0 and i >= 2:
                    V.wait_ge(dSt.h, dSt.n)
                psl = psP[:, sl * 1024:(sl + 1) * 1024]
                VC(V.tensor_add(xpsb[:, sl * 1024:(sl + 1) * 1024], psl,
                                pebuf[:, j * 1024:(j + 1) * 1024]))
                V.tensor_copy(
                    xpT[:].rearrange("p (c s) -> p c s", s=S)[:, :, i * 128:(i + 1) * 128],
                    xpsb[:, sl * 1024:(sl + 1) * 1024].rearrange("p (c s) -> p c s", s=128))
                if half == 0:
                    ve = vS.inc(V.tensor_copy(xnbf[:, sl * 1024:(sl + 1) * 1024], psl))
                    SY.wait_ge(vS.h, ve)
                    dSt.inc(SY.dma_start(
                        out=xp_st[:, :, i * 128:(i + 1) * 128],
                        in_=xpsb[:, sl * 1024:(sl + 1) * 1024].rearrange(
                            "p (c s) -> p c s", s=128)), 16)
                    dSt.inc(SY.dma_start(
                        out=xn_st[:, :, i * 128:(i + 1) * 128],
                        in_=xnbf[:, sl * 1024:(sl + 1) * 1024].rearrange(
                            "p (c s) -> p c s", s=128)), 16)
                    v_tile_done[i] = ve
                else:
                    v_tile_done[i] = vS.inc(V.tensor_copy(xnbf[:, 0:8], psl[:, 0:8]))
        SY.wait_ge(dSt.h, dSt.n)
        nc.all_engine_barrier()
        p1s.close()
        if _NPHASE < 2:
            stage1.close()
            return nc

        p2s = ExitStack()
        sbp2 = mk_sb(p2s)
        KTg2 = sbp2("KTg2", [128, 2 * 2 * S], BF16)
        Vg2 = sbp2("Vg2", [128, 2 * NTK * 256], BF16)
        QTg2 = sbp2("QTg2", [128, 2 * 2 * T], BF16)
        expP = sbp2("expP", [128, 6 * 1024], BF16)
        sums_stg = sbp2("sums_stg", [1, 2048], F32)
        otstg = sbp2("otstg", [128, 1024], BF16)
        wqT = sbp2("wqT", [128, 2048], BF16)
        wkT = sbp2("wkT", [128, 2048], BF16)
        wvT = sbp2("wvT", [128, 2048], BF16)
        if _P2STOP == "palloc":
            nc.all_engine_barrier()
            p2s.close(); stage1.close()
            return nc
        wprep_state = {"par": 0, "evac": {}}

        def gsl(buf, slot, width):
            return slot * width

        def wslice_prep(wdram, rows_lo, dst_fn):
            par = wprep_state["par"]
            wprep_state["par"] += 1
            sl = par % 2
            if par >= 2:
                SY.wait_ge(aS.h, wprep_state["evac"][par - 2])
            cnt = dmain(wst[:, sl * 2048:(sl + 1) * 2048].rearrange(
                            "p (c k) -> p c k", k=1024),
                        wdram[rows_lo:rows_lo + 256, :].rearrange(
                            "(c p) k -> p c k", p=128))
            TE.wait_ge(dIn.h, cnt)
            TE.wait_ge(vS.h, vS.n)
            for c in range(2):
                TE.wait_ge(aS.h, aS.n)
                for kb in range(DC):
                    last = TE.transpose(
                        psP[:, kb * 128:(kb + 1) * 128],
                        wst[:, sl * 2048 + c * 1024 + kb * 128:
                            sl * 2048 + c * 1024 + (kb + 1) * 128],
                        idf[:])
                pS.inc(last)
                SC.wait_ge(pS.h, pS.n)
                for kb in range(DC):
                    a_last = SC.copy(dst_fn(kb, c), psP[:, kb * 128:(kb + 1) * 128])
                aS.inc(a_last)
            wprep_state["evac"][par] = aS.n

        fq = {"v": 0, "a": 0}

        def feed_wslice(wdram, rows_lo, dst):
            par = wprep_state["par"]
            wprep_state["par"] += 1
            sl = par % 2
            if par >= 2:
                SY.wait_ge(aS.h, wprep_state["evac"][par - 2])
            cnt = dmain(wst[:, sl * 2048:(sl + 1) * 2048].rearrange(
                            "p (c k) -> p c k", k=1024),
                        wdram[rows_lo:rows_lo + 256, :].rearrange(
                            "(c p) k -> p c k", p=128))
            yield
            for c in range(2):
                for sub in range(2):
                    TE.wait_ge(dIn.h, cnt)
                    TE.wait_ge(vS.h, fq["v"])
                    TE.wait_ge(aS.h, fq["a"])
                    for b in range(4):
                        kb = sub * 4 + b
                        last = TE.transpose(
                            psR[:, b * 128:(b + 1) * 128],
                            wst[:, sl * 2048 + c * 1024 + kb * 128:
                                sl * 2048 + c * 1024 + (kb + 1) * 128],
                            idf[:])
                    pS.inc(last)
                    SC.wait_ge(pS.h, pS.n)
                    for b in range(4):
                        kb = sub * 4 + b
                        a_last = SC.copy(dst[:, kb * 256 + c * 128: kb * 256 + (c + 1) * 128],
                                         psR[:, b * 128:(b + 1) * 128])
                    fq["a"] = aS.inc(a_last)
                    yield
            wprep_state["evac"][par] = aS.n

        def feed_proj(lhsT_fn, rhs_fn, n_cols, evac_fn):
            TE.wait_ge(vS.h, fq["v"])
            TE.wait_ge(aS.h, fq["a"])
            for kb in range(4):
                TE.matmul(psR[:, 0:n_cols], lhsT_fn(kb), rhs_fn(kb),
                          start=(kb == 0), stop=False, skip_group_check=True)
            yield
            for kb in range(4, DC):
                last = TE.matmul(psR[:, 0:n_cols], lhsT_fn(kb), rhs_fn(kb),
                                 start=False, stop=(kb == DC - 1),
                                 skip_group_check=True)
            mmark = pS.inc(last)
            V.wait_ge(pS.h, mmark)
            fq["v"] = vS.inc(evac_fn(psR))
            yield

        def make_group_feeder(gn):
            slot = gn % 2
            qo = gsl(QTg2, slot, 2 * T)
            ko = gsl(KTg2, slot, 2 * S)
            vo = gsl(Vg2, slot, NTK * 256)
            for c in range(2):
                for n in range(NQ):
                    def evq(ps_t, c=c, n=n):
                        return V.tensor_scalar(
                            QTg2[:, qo + c * T + n * 512: qo + c * T + (n + 1) * 512],
                            ps_t[:, 0:512], SMD,
                            biasT[:, BQ + 2 * gn + c:BQ + 2 * gn + c + 1],
                            op0=ALU.mult, op1=ALU.add)
                    yield from feed_proj(
                        lambda kb, c=c: wqT[:, kb * 256 + c * 128:kb * 256 + (c + 1) * 128],
                        lambda kb, n=n: xpT[:, kb * S + n * 512: kb * S + (n + 1) * 512],
                        512, evq)
            for c in range(2):
                for n in range(8):
                    def evk(ps_t, c=c, n=n):
                        return V.tensor_scalar(
                            KTg2[:, ko + c * S + n * 512: ko + c * S + (n + 1) * 512],
                            ps_t[:, 0:512], 1.0,
                            biasT[:, BK + 2 * gn + c:BK + 2 * gn + c + 1],
                            op0=ALU.mult, op1=ALU.add)
                    yield from feed_proj(
                        lambda kb, c=c: wkT[:, kb * 256 + c * 128:kb * 256 + (c + 1) * 128],
                        lambda kb, n=n: xpT[:, kb * S + n * 512: kb * S + (n + 1) * 512],
                        512, evk)
            for tkc in range(NTK):
                def evv(ps_t, tkc=tkc):
                    return V.scalar_tensor_tensor(
                        Vg2[:, vo + tkc * 256: vo + (tkc + 1) * 256], ps_t[:, 0:256], 0.0,
                        bv_rep[:, gn * 256:(gn + 1) * 256], op0=ALU.add, op1=ALU.add)
                yield from feed_proj(
                    lambda kb, tkc=tkc: xpT[:, kb * S + tkc * 128:kb * S + (tkc + 1) * 128],
                    lambda kb: wvT[:, kb * 256:(kb + 1) * 256],
                    256, evv)

        wslice_prep(wq_d, 0, lambda kb, c: wqT[:, kb * 256 + c * 128: kb * 256 + (c + 1) * 128])
        wslice_prep(wk_d, 0, lambda kb, c: wkT[:, kb * 256 + c * 128: kb * 256 + (c + 1) * 128])
        wslice_prep(wv_d, 0, lambda kb, c: wvT[:, kb * 256 + c * 128: kb * 256 + (c + 1) * 128])
        TE.wait_ge(aS.h, wprep_state["evac"][wprep_state["par"] - 1])
        v_pr_done = {}
        pr_idx = [0]

        def proj_mm(lhsT_fn, rhs_fn, n_cols, evac_fn):
            idx = pr_idx[0]
            pr_idx[0] += 1
            ps_t = psO0 if idx % 2 == 0 else psO1
            TE.wait_ge(vS.h, v_pr_done.get(idx - 2, 0))
            for kb in range(DC):
                last = TE.matmul(ps_t[:, 0:n_cols], lhsT_fn(kb), rhs_fn(kb),
                                 start=(kb == 0), stop=(kb == DC - 1),
                                 skip_group_check=True)
            pS.inc(last)
            V.wait_ge(pS.h, pS.n)
            v_pr_done[idx] = vS.inc(evac_fn(ps_t))

        for c in range(2):
            for n in range(NQ):
                def evq0(ps_t, c=c, n=n):
                    return V.tensor_scalar(
                        QTg2[:, c * T + n * 512: c * T + (n + 1) * 512],
                        ps_t[:, 0:512], SMD, biasT[:, BQ + c:BQ + c + 1],
                        op0=ALU.mult, op1=ALU.add)
                proj_mm(lambda kb, c=c: wqT[:, kb * 256 + c * 128:kb * 256 + (c + 1) * 128],
                        lambda kb, n=n: xpT[:, kb * S + n * 512: kb * S + (n + 1) * 512],
                        512, evq0)
        for c in range(2):
            for n in range(8):
                def evk0(ps_t, c=c, n=n):
                    return V.tensor_scalar(
                        KTg2[:, c * S + n * 512: c * S + (n + 1) * 512],
                        ps_t[:, 0:512], 1.0, biasT[:, BK + c:BK + c + 1],
                        op0=ALU.mult, op1=ALU.add)
                proj_mm(lambda kb, c=c: wkT[:, kb * 256 + c * 128:kb * 256 + (c + 1) * 128],
                        lambda kb, n=n: xpT[:, kb * S + n * 512: kb * S + (n + 1) * 512],
                        512, evk0)
        for tkc in range(NTK):
            def evv0(ps_t, tkc=tkc):
                return V.scalar_tensor_tensor(
                    Vg2[:, tkc * 256:(tkc + 1) * 256], ps_t[:, 0:256], 0.0,
                    bv_rep[:, 0:256], op0=ALU.add, op1=ALU.add)
            proj_mm(lambda kb, tkc=tkc: xpT[:, kb * S + tkc * 128:kb * S + (tkc + 1) * 128],
                    lambda kb: wvT[:, kb * 256:(kb + 1) * 256],
                    256, evv0)

        d_ost_mark = {}
        d_sums_mark = {}
        for g in range(G):
            slot = g % 2
            qo = gsl(QTg2, slot, 2 * T)
            ko = gsl(KTg2, slot, 2 * S)
            vo = gsl(Vg2, slot, NTK * 256)
            if g + 1 < G:
                wslice_prep(wq_d, (g + 1) * 256,
                            lambda kb, c: wqT[:, kb * 256 + c * 128: kb * 256 + (c + 1) * 128])
                wslice_prep(wk_d, (g + 1) * 256,
                            lambda kb, c: wkT[:, kb * 256 + c * 128: kb * 256 + (c + 1) * 128])
                wslice_prep(wv_d, (g + 1) * 256,
                            lambda kb, c: wvT[:, kb * 256 + c * 128: kb * 256 + (c + 1) * 128])
                TE.wait_ge(aS.h, wprep_state["evac"][wprep_state["par"] - 1])
            TE.wait_ge(vS.h, vS.n)
            feeder = make_group_feeder(g + 1) if g + 1 < G else iter(())

            a_exp = {}
            p_qk = {}
            p_av = {}
            v_oev = {-1: 0}
            NI = 2 * NTK
            LAG = 3
            NTOT = NQ * NI

            def emit_qk(ii, qo=qo, ko=ko):
                q, i = divmod(ii, NI)
                tk, p = divmod(i, 2)
                sl2 = ii % 2
                if ii >= 2:
                    TE.wait_ge(aS.h, a_exp[ii - 2])
                for e in range(2):
                    last = TE.matmul(
                        psP[:, sl2 * 1024 + e * 512: sl2 * 1024 + (e + 1) * 512],
                        KTg2[64 * e:64 * e + 64,
                             ko + p * S + tk * 128: ko + p * S + (tk + 1) * 128],
                        QTg2[64 * e:64 * e + 64,
                             qo + p * T + q * 512: qo + p * T + (q + 1) * 512],
                        start=True, stop=True, tile_position=(64 * e, 0),
                        skip_group_check=True)
                p_qk[ii] = pS.inc(last)

            def emit_av(jj_g, vo=vo, g=g):
                q, j = divmod(jj_g, NI)
                tkj, pj = divmod(j, 2)
                eb = (jj_g % 6) * 1024
                if jj_g + LAG + 1 >= NTOT:
                    TE.wait_ge(aS.h, a_exp[jj_g])
                if tkj == 0:
                    TE.wait_ge(vS.h, v_oev[q - 1])
                pso = psO0 if pj == 0 else psO1
                for e in range(2):
                    TE.matmul(
                        pso[64 * e:64 * e + 64, 0:512],
                        Vg2[:, vo + tkj * 256 + (2 * pj + e) * 64:
                            vo + tkj * 256 + (2 * pj + e + 1) * 64],
                        expP[:, eb + e * 512: eb + (e + 1) * 512],
                        start=(tkj == 0), stop=(tkj == NTK - 1),
                        tile_position=(0, 64 * e), skip_group_check=True)
                for e in range(2):
                    hh = 2 * pj + e
                    last = TE.matmul(
                        psS[32 * hh:32 * hh + 1, 0:512], invom_col[:, 0:1],
                        expP[:, eb + e * 512: eb + (e + 1) * 512],
                        start=(tkj == 0), stop=(tkj == NTK - 1),
                        tile_position=(0, 32 * hh), skip_group_check=True)
                p_av[jj_g] = pS.inc(last)
                if j == NI - 1:
                    emit_epilogue(q, g)

            def emit_epilogue(q, g):
                V.wait_ge(pS.h, p_av[q * NI + NI - 1])
                V.wait_ge(dSt.h, d_ost_mark.get((g, q - 1), d_ost_mark.get((g - 1, NQ - 1), 0)))
                V.tensor_copy(otstg[:, 0:512], psO0[:, 0:512])
                V.tensor_copy(otstg[:, 512:1024], psO1[:, 0:512])
                for hh in range(4):
                    last = V.tensor_copy(sums_stg[0:1, hh * 512:(hh + 1) * 512],
                                         psS[32 * hh:32 * hh + 1, 0:512])
                v_oev[q] = vS.inc(last)
                SY.wait_ge(vS.h, v_oev[q])
                dSt.inc(SY.dma_start(
                    out=o_st[:, 2 * g:2 * g + 2, q * 512:(q + 1) * 512],
                    in_=otstg[:, 0:1024].rearrange("p (c s) -> p c s", s=512)), 16)
                dSt.inc(SY.dma_start(out=sums_st[g:g + 1, q, :],
                                     in_=sums_stg[0:1, 0:2048]), 16)
                d_ost_mark[(g, q)] = dSt.n

            def emit_exp(ii):
                sl2 = ii % 2
                SC.wait_ge(pS.h, p_qk[ii])
                a_exp[ii] = aS.inc(SC.activation(
                    expP[:, (ii % 6) * 1024:(ii % 6 + 1) * 1024],
                    psP[:, sl2 * 1024:(sl2 + 1) * 1024], AF.Exp))

            SC.wait_ge(pS.h, pS.n)
            emit_qk(0)
            emit_qk(1)
            for ii in range(NTOT):
                if ii >= LAG:
                    emit_av(ii - LAG)
                if ii % 2 == 0:
                    next(feeder, None)
                emit_exp(ii)
                if ii + 2 < NTOT:
                    emit_qk(ii + 2)
            for jj_g in range(NTOT - LAG, NTOT):
                emit_av(jj_g)
            for _ in feeder:
                pass
            TE.wait_ge(vS.h, v_oev[NQ - 1])
        nc.all_engine_barrier()
        p2s.close()
        stage1.close()
        if _NPHASE < 3:
            return nc

        p3s = ExitStack()
        sbp3 = mk_sb(p3s)
        w1bf = sbp3("w1bf", [128, DC * F], BF16)
        w2bf = sbp3("w2bf", [128, FC * D], BF16)
        h1bf = sbp3("h1bf", [128, FC * FT], BF16)
        xnt = sbp3("xnt", [128, 2 * DC * FT], BF16)
        gt = sbp3("gt", [128, DC * FT], F32)
        wc = {}
        for idx in range(2 * DC):
            kb, hhalf = idx // 2, idx % 2
            sl = idx % 2
            if idx >= 2:
                SY.wait_ge(vS.h, wc[idx - 2])
            cnt = dmain(wst[:, sl * 2048:sl * 2048 + 2048],
                        w1_d[kb * 128:(kb + 1) * 128, hhalf * 2048:hhalf * 2048 + 2048])
            V.wait_ge(dIn.h, cnt)
            wc[idx] = vS.inc(V.tensor_copy(
                w1bf[:, kb * F + hhalf * 2048: kb * F + hhalf * 2048 + 2048],
                wst[:, sl * 2048:sl * 2048 + 2048]))
        wc2 = {}
        for kb in range(FC):
            sl = kb % 2
            SY.wait_ge(vS.h, wc2[kb - 2] if kb >= 2 else wc[2 * DC - 1])
            cnt = dmain(wst[:, sl * 2048:sl * 2048 + 1024],
                        w2_d[kb * 128:(kb + 1) * 128, :])
            V.wait_ge(dIn.h, cnt)
            wc2[kb] = vS.inc(V.tensor_copy(w2bf[:, kb * D:(kb + 1) * D],
                                           wst[:, sl * 2048:sl * 2048 + 1024]))
        SY.wait_ge(dSt.h, dSt.n)
        nc.all_engine_barrier()

        v_h1ev = {}
        a_gev = {}
        d_gst = {}
        d_xnt = {}
        p_h1last = {}
        for tc in range(T // FT):
            sl = tc % 2
            if tc >= 2:
                SY.wait_ge(pS.h, p_h1last[tc - 2])
            d_xnt[tc] = dmain(xnt[:, sl * DC * FT:(sl + 1) * DC * FT],
                              xn_st[:, :, tc * FT:(tc + 1) * FT])
            TE.wait_ge(dIn.h, d_xnt[tc])
            for ic in range(FC):
                ph = (psO0 if ic % 2 == 0 else psO1)[:, 0:256]
                TE.wait_ge(vS.h, v_h1ev.get((tc, ic - 2), 0))
                for kb in range(DC):
                    last = TE.matmul(
                        ph, w1bf[:, kb * F + ic * 128: kb * F + (ic + 1) * 128],
                        xnt[:, sl * DC * FT + kb * FT: sl * DC * FT + (kb + 1) * FT],
                        start=(kb == 0), stop=(kb == DC - 1), skip_group_check=True)
                pS.inc(last)
                V.wait_ge(pS.h, pS.n)
                v_h1ev[(tc, ic)] = vS.inc(V.tensor_scalar(
                    h1bf[:, ic * FT:(ic + 1) * FT], ph, b1T[:, ic:ic + 1], 0.0,
                    op0=ALU.add, op1=ALU.max))
                if ic == FC - 1:
                    p_h1last[tc] = pS.n
            for jc in range(DC):
                ph = (psS if jc % 2 == 0 else psR)[:, 0:256]
                TE.wait_ge(aS.h, a_gev.get((tc, jc - 2), 0))
                if jc == 0:
                    TE.wait_ge(vS.h, v_h1ev[(tc, FC - 1)])
                for kb in range(FC):
                    last = TE.matmul(
                        ph, w2bf[:, kb * D + jc * 128: kb * D + (jc + 1) * 128],
                        h1bf[:, kb * FT:(kb + 1) * FT],
                        start=(kb == 0), stop=(kb == FC - 1), skip_group_check=True)
                pS.inc(last)
                SC.wait_ge(pS.h, pS.n)
                if tc >= 1:
                    SC.wait_ge(dSt.h, d_gst[tc - 1])
                a_gev[(tc, jc)] = aS.inc(SC.activation(
                    gt[:, jc * FT:(jc + 1) * FT], ph, AF.Silu,
                    bias=biasT[:, B2 + jc:B2 + jc + 1]))
            SY.wait_ge(aS.h, a_gev[(tc, DC - 1)])
            dSt.inc(SY.dma_start(out=g_st[:, :, tc * FT:(tc + 1) * FT],
                                 in_=gt[:].rearrange("p (c s) -> p c s", s=FT)), 16)
            d_gst[tc] = dSt.n
        SY.wait_ge(dSt.h, dSt.n)
        nc.all_engine_barrier()
        p3s.close()
        if _NPHASE < 4:
            return nc

        p4s = ExitStack()
        sbp4 = mk_sb(p4s)
        woT = sbp4("woT", [128, DC * 1024], BF16)
        xpt4 = sbp4("xpt4", [128, DC * 512], F32)
        ot4 = sbp4("ot4", [128, DC * 512], BF16)
        otn = sbp4("otn", [128, DC * 512], BF16)
        sel_sb = sbp4("sel_sb", [16, 1024], F32)
        sumsb = sbp4("sumsb", [16, 512], F32)
        recipb = sbp4("recipb", [16, 512], F32)
        gt4 = sbp4("gt4", [128, DC * 512], F32)
        abuf = sbp4("abuf", [128, DC * 512], F32)
        asq = sbp4("asq", [128, DC * 512], F32)
        sbuf_s = sbp4("sbuf_s", [128, DC * 512], F32)
        st4 = sbp4("st4", [1, 6 * 512], F32)
        xt4 = sbp4("xt4", [128, 2 * 1024], F32)
        outb = sbp4("outb", [128, 2 * 1024], F32)
        for rg in range(4):
            if rg > 0:
                SC.wait_ge(vS.h, vS.n)
            wslice_prep(wo_d, rg * 256,
                        lambda kb, c, rg=rg: woT[:, kb * 1024 + rg * 256 + c * 128:
                                                 kb * 1024 + rg * 256 + (c + 1) * 128])
        SY.wait_ge(dSt.h, dSt.n)
        nc.all_engine_barrier()

        out_dma_mark = {}
        cnt_sel = dmain(sel_sb[:], sel_d)
        for q in range(NQ):
            if q > 0:
                SY.wait_ge(pS.h, pS.n)
                SY.wait_ge(vS.h, vS.n)
            c_o = dRe.n + 16
            dRe.inc(SY.dma_start(out=ot4[:].rearrange("p (c s) -> p c s", s=512),
                                 in_=o_st[:, :, q * 512:(q + 1) * 512]), 16)
            for g2 in range(G):
                dRe.inc(SY.dma_start(
                    out=sumsb[4 * g2:4 * g2 + 4, :],
                    in_=sums_st[g2, q, :].rearrange("(j t) -> j t", t=512)), 16)
            c_s = dRe.n
            dRe.inc(SY.dma_start(out=xpt4[:].rearrange("p (c s) -> p c s", s=512),
                                 in_=xp_st[:, :, q * 512:(q + 1) * 512]), 16)
            dRe.inc(SY.dma_start(out=gt4[:].rearrange("p (c s) -> p c s", s=512),
                                 in_=g_st[:, :, q * 512:(q + 1) * 512]), 16)
            d_re = dRe.n
            V.wait_ge(dRe.h, c_s)
            v_rec4 = VC(V.reciprocal(recipb[:], sumsb[:]))
            TE.wait_ge(vS.h, v_rec4)
            TE.wait_ge(dRe.h, c_o)
            TE.wait_ge(dIn.h, cnt_sel)
            v_nrm = {}
            for kb in range(DC):
                pn = psR if kb % 2 == 0 else psS
                TE.wait_ge(vS.h, v_nrm.get(kb - 2, 0))
                last = TE.matmul(pn[:, 0:512], sel_sb[0:16, kb * 128:(kb + 1) * 128],
                                 recipb[:], start=True, stop=True,
                                 skip_group_check=True)
                pS.inc(last)
                V.wait_ge(pS.h, pS.n)
                v_nrm[kb] = vS.inc(V.tensor_mul(
                    otn[:, kb * 512:(kb + 1) * 512],
                    ot4[:, kb * 512:(kb + 1) * 512], pn[:, 0:512]))
            v_aev = {}
            for jc in range(DC):
                ph = psO0 if jc % 2 == 0 else psO1
                TE.wait_ge(vS.h, v_aev.get(jc - 2, 0))
                if jc == 0:
                    TE.wait_ge(vS.h, v_nrm[DC - 1])
                for kb in range(DC):
                    last = TE.matmul(
                        ph[:, 0:512],
                        woT[:, kb * 1024 + jc * 128: kb * 1024 + (jc + 1) * 128],
                        otn[:, kb * 512:(kb + 1) * 512],
                        start=(kb == 0), stop=(kb == DC - 1), skip_group_check=True)
                pS.inc(last)
                V.wait_ge(pS.h, pS.n)
                if jc == 0:
                    V.wait_ge(dRe.h, d_re)
                v_aev[jc] = VC(V.scalar_tensor_tensor(
                    abuf[:, jc * 512:(jc + 1) * 512], ph[:, 0:512],
                    biasT[:, BO + jc:BO + jc + 1], xpt4[:, jc * 512:(jc + 1) * 512],
                    op0=ALU.add, op1=ALU.add))
            v_asq = VC(V.tensor_mul(asq[:], abuf[:], abuf[:]))
            TE.wait_ge(vS.h, v_asq)
            for kb in range(DC):
                TE.matmul(psS[0:1, 0:512], onescol_f[:, 0:1],
                          abuf[:, kb * 512:(kb + 1) * 512],
                          start=(kb == 0), stop=(kb == DC - 1),
                          tile_position=(0, 0), skip_group_check=True)
                last = TE.matmul(psS[32:33, 0:512], onescol_f[:, 0:1],
                                 asq[:, kb * 512:(kb + 1) * 512],
                                 start=(kb == 0), stop=(kb == DC - 1),
                                 tile_position=(0, 32), skip_group_check=True)
            p_stats = pS.inc(last)
            V.wait_ge(pS.h, p_stats)
            VC(V.tensor_scalar_mul(st4[0:1, 0:512], psS[0:1, 0:512], 1.0 / 1024.0))
            VC(V.tensor_scalar_mul(st4[0:1, 512:1024], psS[32:33, 0:512], 1.0 / 1024.0))
            VC(V.tensor_mul(st4[0:1, 1024:1536], st4[0:1, 0:512], st4[0:1, 0:512]))
            vv = VC(V.tensor_sub(st4[0:1, 1024:1536], st4[0:1, 512:1024],
                                 st4[0:1, 1024:1536]))
            SC.wait_ge(vS.h, vv)
            AC(SC.activation(st4[0:1, 1536:2048], st4[0:1, 1024:1536], AF.Ln,
                             bias=EPS2[0:1, 0:1]))
            a_r = AC(SC.activation(st4[0:1, 2048:2560], st4[0:1, 1536:2048],
                                   AF.Exp, scale=-0.5))
            V.wait_ge(aS.h, a_r)
            v_mr = VC(V.tensor_mul(st4[0:1, 2560:3072], st4[0:1, 0:512],
                                   st4[0:1, 2048:2560]))
            TE.wait_ge(vS.h, v_mr)
            for e in range(2):
                TE.matmul(psR[64 * e:64 * e + 64, 0:512], onesrow_f[0:1, 0:64],
                          st4[0:1, 2048:2560], start=True, stop=True,
                          tile_position=(0, 64 * e), skip_group_check=True)
                last = TE.matmul(psP[64 * e:64 * e + 64, 0:512], onesrow_f[0:1, 0:64],
                                 st4[0:1, 2560:3072], start=True, stop=True,
                                 tile_position=(0, 64 * e), skip_group_check=True)
            p_rep = pS.inc(last)
            V.wait_ge(pS.h, p_rep)
            for jc in range(DC):
                VC(V.tensor_mul(sbuf_s[:, jc * 512:(jc + 1) * 512],
                                abuf[:, jc * 512:(jc + 1) * 512], psR[:, 0:512]))
                VC(V.tensor_sub(sbuf_s[:, jc * 512:(jc + 1) * 512],
                                sbuf_s[:, jc * 512:(jc + 1) * 512], psP[:, 0:512]))
                VC(V.tensor_scalar(sbuf_s[:, jc * 512:(jc + 1) * 512],
                                   sbuf_s[:, jc * 512:(jc + 1) * 512],
                                   biasT[:, GA + jc:GA + jc + 1],
                                   biasT[:, BE + jc:BE + jc + 1],
                                   op0=ALU.mult, op1=ALU.add))
                VC(V.tensor_mul(sbuf_s[:, jc * 512:(jc + 1) * 512],
                                sbuf_s[:, jc * 512:(jc + 1) * 512],
                                gt4[:, jc * 512:(jc + 1) * 512]))
            v_sg = vS.n
            TE.wait_ge(vS.h, v_sg)
            for sub in range(4):
                i_tok = q * 512 + sub * 128
                sl = sub % 2
                if sub >= 2 or q > 0:
                    SY.wait_ge(vS.h, vS.n)
                cx = dmain(xt4[:, sl * 1024:(sl + 1) * 1024],
                           x_seq[i_tok:i_tok + 128, :])
                if sub >= 1 or q > 0:
                    TE.wait_ge(vS.h, vS.n)
                for c in range(DC):
                    last = TE.transpose(
                        psP[:, 1024 + c * 128: 1024 + (c + 1) * 128],
                        sbuf_s[:, c * 512 + sub * 128: c * 512 + sub * 128 + 128],
                        idf[:])
                p_tp = pS.inc(last)
                V.wait_ge(pS.h, p_tp)
                V.wait_ge(dIn.h, cx)
                sub_g = q * 4 + sub
                if sub_g >= 2:
                    V.wait_ge(dOut.h, out_dma_mark[sub_g - 2])
                v_out = vS.inc(V.tensor_add(outb[:, sl * 1024:(sl + 1) * 1024],
                                            psP[:, 1024:2048],
                                            xt4[:, sl * 1024:(sl + 1) * 1024]))
                SY.wait_ge(vS.h, v_out)
                dOut.inc(SY.dma_start(out=y_out[i_tok:i_tok + 128, :],
                                      in_=outb[:, sl * 1024:(sl + 1) * 1024]), 16)
                out_dma_mark[sub_g] = dOut.n
        SY.wait_ge(dOut.h, dOut.n)
        nc.all_engine_barrier()
        p4s.close()
    return nc


def _host_constants():
    i_even = np.arange(0, D, 2, dtype=np.float64)
    base = np.power(10000.0, 2.0 * i_even / D)
    ang0 = np.arange(S, dtype=np.float64)[:, None] / base[None, :]
    didx = np.arange(D) // 2
    full = ang0[:, didx]
    a1 = {}
    for halfswap in range(2):
        perm = (np.arange(S) if halfswap == 0
                else np.concatenate([np.arange(T, S), np.arange(0, T)]))
        arr = full[perm].T.reshape(DC, 128, S).transpose(1, 0, 2)
        a1[halfswap] = np.ascontiguousarray(arr.astype(np.float32))
    h2col = (0.25 * (np.arange(128) % 2)).astype(np.float32).reshape(128, 1)
    idf = np.eye(128, dtype=np.float32)
    sel = np.zeros((16, 1024), dtype=np.float32)
    for c in range(DC):
        for p in range(128):
            sel[2 * c + p // 64, c * 128 + p] = 1.0
    return a1, h2col, idf, sel


def kernel(L0_attn_x, L0_attn_rby, L0_attn_Wq, L0_attn_bq, L0_attn_Wk, L0_attn_bk,
           L0_attn_Wv, L0_attn_bv, L0_attn_Wo, L0_attn_bo, L0_attn_ln_gamma,
           L0_attn_ln_beta, L0_ffn_W1, L0_ffn_b1, L0_ffn_W2, L0_ffn_b2):
    x = np.ascontiguousarray(np.asarray(L0_attn_x, dtype=np.float32))
    a1, h2col, idf, sel = _host_constants()
    nc = build_nc()

    def row(v):
        return np.ascontiguousarray(np.asarray(v, np.float32).reshape(1, -1))

    shared = {
        "h2col": h2col, "idf": idf, "sel": sel,
        "rby": row(L0_attn_rby),
        "wq": np.ascontiguousarray(L0_attn_Wq, np.float32),
        "wk": np.ascontiguousarray(L0_attn_Wk, np.float32),
        "wv": np.ascontiguousarray(L0_attn_Wv, np.float32),
        "wo": np.ascontiguousarray(L0_attn_Wo, np.float32),
        "w1": np.ascontiguousarray(L0_ffn_W1, np.float32),
        "w2": np.ascontiguousarray(L0_ffn_W2, np.float32),
        "bq": row(L0_attn_bq), "bk": row(L0_attn_bk), "bv": row(L0_attn_bv),
        "bo": row(L0_attn_bo), "b1": row(L0_ffn_b1), "b2": row(L0_ffn_b2),
        "gam": row(L0_attn_ln_gamma), "bet": row(L0_attn_ln_beta),
    }
    in_maps = []
    for c in range(8):
        b, half = c // 2, c % 2
        own = x[b, half * T:(half + 1) * T]
        oth = x[b, (1 - half) * T:(2 - half) * T]
        m = dict(shared)
        m["x_seq"] = np.ascontiguousarray(np.concatenate([own, oth], axis=0))
        m["a1c"] = a1[half]
        in_maps.append(m)

    res = run_bass_kernel_spmd(nc, in_maps, core_ids=list(range(8)))
    out = np.empty((4, S, D), dtype=np.float32)
    for c in range(8):
        b, half = c // 2, c % 2
        out[b, half * T:(half + 1) * T] = res.results[c]["y"]
    return out
